# version 38
# baseline (speedup 1.0000x reference)
import numpy as np
import ml_dtypes
from contextlib import ExitStack

import concourse.bass as bass
import concourse.mybir as mybir
from concourse.bass_utils import run_bass_kernel_spmd

F32 = mybir.dt.float32
BF16 = mybir.dt.bfloat16
I32 = mybir.dt.int32
U8 = mybir.dt.uint8
AF = mybir.ActivationFunctionType
ALU = mybir.AluOpType

SEQ, EMB, NHEAD, DHEAD, HID = 4096, 1024, 16, 64, 3072
NPAIR, NEC, NHC, NDC = 8, 8, 24, 8
NQ = 2048
CT = (8, 16, 24, 32)
TILES = ((0, 2, 5, 7), (1, 3, 4, 6))
NTC = SEQ // 128
SCALE = 0.125
FS2 = 1.0 / (127.0 * 127.0)


class MK:
    def __init__(self, taps=()):
        self.taps = set(taps)
        self.nc = bass.Bass("TRN2", target_bir_lowering=False)
        self.es = ExitStack()
        self.sems = {}
        self.c = {}
        self.steps = []
        self.lw = {}
        self.tap_outs = {}

    def sem(self, name):
        if name not in self.sems:
            self.sems[name] = self.es.enter_context(self.nc.semaphore(name))
            self.c[name] = 0
        return self.sems[name]

    def st(self, eng, fn, waits=(), inc=None):
        ws = []
        for s, v in waits:
            if v is not None and v > 0:
                self.sem(s)
                ws.append((s, v))
        if inc:
            self.sem(inc[0])
        self.steps.append((eng, tuple(ws), fn, inc))
        if inc:
            self.c[inc[0]] += inc[1]
            return self.c[inc[0]]
        return None

    def flush(self):
        steps, self.steps = self.steps, []
        nc, sems, lw = self.nc, self.sems, self.lw

        def runner(engname):
            def body(eng):
                for en, ws, fn, inc in steps:
                    if en != engname:
                        continue
                    for s, v in ws:
                        k = (en, s)
                        if lw.get(k, 0) >= v:
                            continue
                        eng.wait_ge(sems[s], v)
                        lw[k] = v
                    inst = fn(eng)
                    if inc is not None:
                        inst.then_inc(sems[inc[0]], inc[1])
            return body

        with nc.Block() as blk:
            blk.sync(runner('sp'))
            blk.scalar(runner('act'))
            blk.tensor(runner('pe'))
            blk.vector(runner('dve'))
            blk.gpsimd(runner('pool'))

    def tap(self, name, ap, shape, dt, waits=()):
        if name not in self.taps:
            return
        t = self.nc.dram_tensor('dbg_' + name, list(shape), dt, kind="ExternalOutput").ap()
        self.tap_outs[name] = t
        cnt = self.st('sp', lambda e, t=t, ap=ap: e.dma_start(t, ap), waits=waits,
                      inc=('s_tap', 16))
        self.st('sp', lambda e: e.nop(), waits=[('s_tap', cnt)])


def build_full(taps=()):
    mk = MK(taps)
    nc = mk.nc
    st = mk.st

    d = {}
    for name, shape, dt in [
        ('x', (SEQ, EMB), F32), ('xq', (NQ, EMB), F32),
        ('wq', (NPAIR, 128, NEC, 128), F32), ('wk', (NPAIR, 128, NEC, 128), F32),
        ('wv', (4, 128, NEC, 256), F32), ('wo', (NEC, 128, NPAIR, 128), F32),
        ('bq', (128, NPAIR), F32), ('bk', (128, NPAIR), F32),
        ('bo', (128, NEC), F32), ('bv', (1, EMB), F32),
        ('wg', (NHC, 128, NEC, 128), I32), ('wu', (NHC, 128, NEC, 128), I32),
        ('wd', (128, NHC, NDC, 128), I32),
        ('mk', (128, 4, 8, 512), U8),
        ('ident', (128, 128), BF16), ('ones_col', (128, 1), BF16),
        ('ones_row', (128, 128), BF16), ('ones_rf', (128, 128), F32),
    ]:
        d[name] = nc.dram_tensor(name, list(shape), dt, kind="ExternalInput").ap()
    y = nc.dram_tensor('y', [EMB, NQ], F32, kind="ExternalOutput").ap()

    SB = lambda n, sh, dt, side='left': mk.es.enter_context(
        nc.sbuf_tensor(n, list(sh), dt, side=side))
    ident_sb = SB('ident_sb', (128, 128), BF16, 'right')
    onescol = SB('onescol', (128, 1), BF16, 'right')
    onesrow = SB('onesrow', (128, 128), BF16, 'right')
    onesrf = SB('onesrf', (128, 128), F32, 'right')
    bq_sb = SB('bq_sb', (128, NPAIR), F32, 'right')
    bk_sb = SB('bk_sb', (128, NPAIR), F32, 'right')
    bo_sb = SB('bo_sb', (128, NEC), F32, 'right')
    bv_bf = SB('bv_bf', (1, EMB), BF16, 'right')

    o_allT = SB('o_allT', (128, NPAIR, NQ), BF16)
    x1T = o_allT
    s_xq = ExitStack()
    xqT = s_xq.enter_context(nc.sbuf_tensor('xqT', [128, NEC, NQ], BF16))
    s_mk = ExitStack()
    masks_sb = s_mk.enter_context(nc.sbuf_tensor('masks_sb', [128, 4, 8, 512], U8))
    s_xt = ExitStack()
    xT = s_xt.enter_context(nc.sbuf_tensor('xT', [128, NEC, SEQ], BF16))

    s_p0 = ExitStack()
    stg = s_p0.enter_context(nc.sbuf_tensor('stg', [128, 16, 1024], BF16))
    pt_ps = [s_p0.enter_context(nc.psum_tensor(f'pt{i}', [128, NEC, 128], BF16))
             for i in range(2)]

    for nm, dst in [('ident', ident_sb), ('ones_col', onescol), ('ones_row', onesrow),
                    ('ones_rf', onesrf), ('bq', bq_sb), ('bk', bk_sb), ('bo', bo_sb),
                    ('mk', masks_sb)]:
        st('sp', lambda e, s=d[nm], t=dst: e.dma_start(t[:], s), inc=('s_cst', 16))
    cst_total = mk.c['s_cst']
    st('sp', lambda e: e.nop(), waits=[('s_cst', cst_total)])
    st('pool', lambda e: e.dma_start(bv_bf[:], d['bv']), inc=('s_dma0', 16))
    st('pool', lambda e: e.nop(), waits=[('s_dma0', mk.c['s_dma0'])])

    wave_total = {}
    for w in range(3):
        for i in range(16 * w, 16 * w + 16):
            if i < 32:
                src = d['x'][i * 128:(i + 1) * 128, :]
            else:
                src = d['xq'][(i - 32) * 128:(i - 31) * 128, :]
            st('pool', lambda e, s=src, sl=i % 16: e.dma_start(stg[:, sl, :], s),
               waits=[('s_dve0', i - 15)], inc=('s_dma0', 16))
        wave_total[w] = mk.c['s_dma0']
        st('pool', lambda e: e.nop(), waits=[('s_dma0', wave_total[w])])

    for i in range(48):
        par = i % 2
        if i < 32:
            cp = (lambda e, tc=i, p=par: e.tensor_copy(
                xT[:, :, tc * 128:(tc + 1) * 128], pt_ps[p][:]))
        else:
            cp = (lambda e, tc=i - 32, p=par: e.tensor_copy(
                xqT[:, :, tc * 128:(tc + 1) * 128], pt_ps[p][:]))

        def trn(e, p=par, sl=i % 16):
            inst = None
            for ec in range(NEC):
                inst = e.transpose(pt_ps[p][:, ec, :],
                                   stg[:, sl, ec * 128:(ec + 1) * 128], ident_sb[:])
            return inst
        st('pe', trn, waits=[('s_cst', cst_total), ('s_dma0', wave_total[i // 16]),
                             ('s_dve0', i - 1)],
           inc=('s_pe0', 1))
        st('dve', cp, waits=[('s_pe0', i + 1)], inc=('s_dve0', 1))

    mk.tap('xT', xT[:], (128, NEC, SEQ), BF16, waits=[('s_dve0', 48)])
    mk.tap('xqT', xqT[:], (128, NEC, NQ), BF16, waits=[('s_dve0', 48)])
    s_p0.close()

    s_p1 = ExitStack()
    V2 = s_p1.enter_context(nc.sbuf_tensor('V2', [128, NTC, 256], BF16))
    K2T = s_p1.enter_context(nc.sbuf_tensor('K2T', [128, SEQ], BF16))
    Q2T = s_p1.enter_context(nc.sbuf_tensor('Q2T', [128, NQ], BF16))
    PT = s_p1.enter_context(nc.sbuf_tensor('PT', [128, 5, 1024], BF16))
    rb2 = s_p1.enter_context(nc.sbuf_tensor('rb2', [128, 1024], BF16))
    rs_sb = s_p1.enter_context(nc.sbuf_tensor('rs_sb', [65, 2, 512], F32))
    wq_sb = s_p1.enter_context(nc.sbuf_tensor('wq_sb', [128, 2, NEC, 128], BF16))
    wk_sb = s_p1.enter_context(nc.sbuf_tensor('wk_sb', [128, 2, NEC, 128], BF16))
    wv_sb = s_p1.enter_context(nc.sbuf_tensor('wv_sb', [128, 2, NEC, 256], BF16))
    pp = [s_p1.enter_context(nc.psum_tensor(f'pp{i}', [128, 1024], F32)) for i in range(2)]
    ps_o = [s_p1.enter_context(nc.psum_tensor(f'pso{i}', [128, 512], F32))
            for i in range(2)]
    ps_den = [s_p1.enter_context(nc.psum_tensor(f'psd{i}', [65, 512], F32))
              for i in range(2)]

    u = 0
    user_done = []
    pool_cons = []
    g = 0
    pv_snap = {}
    exp_snap = {}
    last_on = [0, 0]
    last_rs = [0, 0]
    exp_list = []
    slot_of = {}
    pv_ready = {}
    pv_done = {}
    last_bcast_user = [0, 0]
    last_rb_on = 0
    kq_users_end = [0] * (NPAIR + 1)
    pv_end_pair = [0] * (NPAIR + 1)
    pending_norm = []

    def emit_norm_tail():
        nonlocal u, last_rb_on
        if not pending_norm:
            return
        opar_, p_, t_, rs1_ = pending_norm.pop(0)
        par = u % 2

        def bcast(e, par=par, opar=opar_):
            e.matmul(pp[par][0:64, 0:512], onesrf[0:1, 0:64], rs_sb[0:1, opar, :],
                     start=True, stop=True)
            return e.matmul(pp[par][64:128, 512:1024], onesrf[64:65, 0:64],
                            rs_sb[64:65, opar, :], start=True, stop=True,
                            tile_position=(64, 64))
        user_done.append(st('pe', bcast,
                            waits=[(f's_plc{u % 2}', u // 2), ('s_dve_rs', rs1_)],
                            inc=('s_pe_user', 1)))
        last_bcast_user[opar_] = user_done[u]
        rb_snap = (f's_plc{u % 2}', st('act', lambda e, par=par: e.activation(
            rb2[:], pp[par][:], AF.Copy),
            waits=[('s_pe_user', user_done[u]), ('s_dve_on', last_rb_on)],
            inc=(f's_plc{u % 2}', 1)))
        u += 1

        st('dve', lambda e, opar=opar_, p=p_, t=t_: e.tensor_mul(
            o_allT[0:64, p, t * 512:(t + 1) * 512], ps_o[opar][0:64, :],
            rb2[0:64, 0:512]),
            waits=[rb_snap], inc=('s_dve_on', 1))
        last_on[opar_] = st('dve', lambda e, opar=opar_, p=p_, t=t_: e.tensor_mul(
            o_allT[64:128, p, t * 512:(t + 1) * 512], ps_o[opar][64:128, :],
            rb2[64:128, 512:1024]),
            inc=('s_dve_on', 1))
        last_rb_on = last_on[opar_]
    wqk_free = [0, 0]
    wv_free = [0, 0]
    wqk_snap = [0] * NPAIR
    wv_snap = [0] * 4

    def pool_boundary(semname):
        cnt = mk.c[semname]
        st('pool', lambda e: e.nop(), waits=[(semname, cnt)])

    def load_wqk(p):
        par = p % 2
        gate = wqk_free[par]
        st('pool', lambda e, p=p, par=par: e.dma_start(wq_sb[:, par], d['wq'][p]),
           waits=[('s_pe_user', gate)], inc=('s_ldw', 16))
        wqk_snap[p] = st('pool', lambda e, p=p, par=par: e.dma_start(wk_sb[:, par], d['wk'][p]),
                         inc=('s_ldw', 16))
        pool_boundary('s_ldw')

    def load_wv(gi):
        par = gi % 2
        gate = wv_free[par]
        wv_snap[gi] = st('pool', lambda e, gi=gi, par=par: e.dma_start(wv_sb[:, par], d['wv'][gi]),
                         waits=[('s_pe_user', gate)], inc=('s_ldw', 16))
        pool_boundary('s_ldw')

    load_wqk(0)
    load_wv(0)

    for p in range(NPAIR):
        ppar = p % 2
        if p + 1 < NPAIR:
            load_wqk(p + 1)
        if p % 2 == 1 and (p + 1) // 2 < 4:
            load_wv((p + 1) // 2)

        if p % 2 == 0:
            gi = p // 2
            gpar = gi % 2
            for tc in range(NTC):
                par = u % 2

                def vmm(e, tc=tc, par=par, gpar=gpar, gi=gi):
                    for ec in range(NEC):
                        e.matmul(pp[par][:, 0:256], xT[:, ec, tc * 128:(tc + 1) * 128],
                                 wv_sb[:, gpar, ec, :], start=(ec == 0), stop=False)
                    return e.matmul(pp[par][:, 0:256], onesrow[0:1, :],
                                    bv_bf[0:1, gi * 256:(gi + 1) * 256],
                                    start=False, stop=True)
                vw = [(f's_plc{u % 2}', u // 2), ('s_ldw', wv_snap[gi])]
                if gi == 0 and tc == 0:
                    vw.append(('s_dve0', 48))
                user_done.append(st('pe', vmm, waits=vw, inc=('s_pe_user', 1)))
                pool_cons.append(st('act', lambda e, tc=tc, par=par: e.activation(
                    V2[:, tc, :], pp[par][:, 0:256], AF.Copy),
                    waits=[('s_pe_user', user_done[u]),
                           ('s_pe_pv', pv_end_pair[max(p - 1, 0)])],
                    inc=(f's_plc{u % 2}', 1)))
                u += 1
            wv_free[gpar] = user_done[-1]

        for ts in range(8):
            par = u % 2

            def kmm(e, ts=ts, par=par, ppar=ppar):
                inst = None
                for ec in range(NEC):
                    inst = e.matmul(pp[par][:, 0:512], wk_sb[:, ppar, ec, :],
                                    xT[:, ec, ts * 512:(ts + 1) * 512],
                                    start=(ec == 0), stop=(ec == NEC - 1))
                return inst
            user_done.append(st('pe', kmm,
                                waits=[(f's_plc{u % 2}', u // 2), ('s_ldw', wqk_snap[p])],
                                inc=('s_pe_user', 1)))
            pool_cons.append(st('dve', lambda e, ts=ts, par=par, p=p: e.tensor_scalar(
                K2T[:, ts * 512:(ts + 1) * 512], pp[par][:, 0:512],
                bk_sb[:, p:p + 1], None, ALU.add),
                waits=[('s_pe_user', user_done[u]),
                       ('s_pe_user', kq_users_end[max(p - 1, 0)])],
                inc=(f's_plc{u % 2}', 1)))
            u += 1
            if ts in (2, 4):
                emit_norm_tail()
        if p == NPAIR - 1:
            xT_last_user = mk.c['s_pe_user']

        for qs in range(4):
            par = u % 2

            def qmm(e, qs=qs, par=par, ppar=ppar):
                inst = None
                for ec in range(NEC):
                    inst = e.matmul(pp[par][:, 0:512], wq_sb[:, ppar, ec, :],
                                    xqT[:, ec, qs * 512:(qs + 1) * 512],
                                    start=(ec == 0), stop=(ec == NEC - 1))
                return inst
            user_done.append(st('pe', qmm,
                                waits=[(f's_plc{u % 2}', u // 2), ('s_ldw', wqk_snap[p])],
                                inc=('s_pe_user', 1)))
            pool_cons.append(st('dve', lambda e, qs=qs, par=par, p=p: e.tensor_scalar(
                Q2T[:, qs * 512:(qs + 1) * 512], pp[par][:, 0:512],
                bq_sb[:, p:p + 1], None, ALU.add),
                waits=[('s_pe_user', user_done[u]),
                       ('s_pe_user', kq_users_end[max(p - 1, 0)])],
                inc=(f's_plc{u % 2}', 1)))
            u += 1
        wqk_free[ppar] = user_done[-1]
        kq_ready = [('s_plc0', mk.c.get('s_plc0', 0)), ('s_plc1', mk.c.get('s_plc1', 0))]

        NSLOT = 5
        for tp in range(2):
            tA, tB = 2 * tp, 2 * tp + 1
            streams = ((tA, CT[tA]), (tB, CT[tB]))
            nmax = CT[tB]
            for j in range(nmax + 3):
                for (t, nkc) in streams:
                    opar = t % 2
                    if j < nkc:
                        kc = j
                        par = u % 2
                        gpt = g % NSLOT

                        def smm(e, kc=kc, t=t, par=par):
                            e.matmul(pp[par][:, 0:512],
                                     K2T[0:64, kc * 128:(kc + 1) * 128],
                                     Q2T[0:64, t * 512:(t + 1) * 512],
                                     start=True, stop=True)
                            return e.matmul(pp[par][:, 512:1024],
                                            K2T[64:128, kc * 128:(kc + 1) * 128],
                                            Q2T[64:128, t * 512:(t + 1) * 512],
                                            start=True, stop=True, tile_position=(64, 0))
                        user_done.append(st('pe', smm,
                                            waits=[(f's_plc{u % 2}', u // 2)] + kq_ready,
                                            inc=('s_pe_user', 1)))
                        exp_waits = [('s_pe_user', user_done[u])]
                        n_exp = len(exp_list)
                        if n_exp >= NSLOT:
                            exp_waits.append(('s_pe_pv', pv_done[exp_list[n_exp - NSLOT]]))
                        pool_cons.append(st('act', lambda e, par=par, gpt=gpt: e.activation(
                            PT[:, gpt, :], pp[par][:], AF.Exp, scale=SCALE),
                            waits=exp_waits, inc=(f's_plc{u % 2}', 1)))
                        es = (f's_plc{u % 2}', mk.c[f's_plc{u % 2}'])
                        exp_list.append((t, kc))
                        slot_of[(t, kc)] = gpt
                        u += 1
                        g += 1

                        rel = kc - (nkc - 8)
                        if rel >= 0:
                            msnap = st('dve', lambda e, gpt=gpt, t=t, rel=rel: e.tensor_mul(
                                PT[:, gpt, :].rearrange("p (h q) -> p h q", h=2),
                                PT[:, gpt, :].rearrange("p (h q) -> p h q", h=2),
                                masks_sb[:, t, rel, :].unsqueeze(1).broadcast_to([128, 2, 512])),
                                waits=[es], inc=('s_mskd', 1))
                            pv_ready[(t, kc)] = ('s_mskd', msnap)
                        else:
                            pv_ready[(t, kc)] = es

                    pk = j - 2
                    if pk < 0 or pk >= nkc:
                        continue
                    pv_waits = [pv_ready[(t, pk)]]
                    if pk == 0:
                        pv_waits += [('s_dve_on', last_on[opar]),
                                     ('s_dve_rs', last_rs[opar])]

                    def pvmm(e, kc=pk, opar=opar, gpt=slot_of[(t, pk)], ppar=ppar,
                             nkc=nkc):
                        first, last = kc == 0, kc == nkc - 1
                        e.matmul(ps_o[opar][0:64, :],
                                 V2[:, kc, ppar * 128:ppar * 128 + 64],
                                 PT[:, gpt, 0:512], start=first, stop=last)
                        e.matmul(ps_o[opar][64:128, :],
                                 V2[:, kc, ppar * 128 + 64:ppar * 128 + 128],
                                 PT[:, gpt, 512:1024], start=first, stop=last,
                                 tile_position=(0, 64), skip_group_check=True)
                        e.matmul(ps_den[opar][0:1, :], onescol[:], PT[:, gpt, 0:512],
                                 start=first, stop=last)
                        return e.matmul(ps_den[opar][64:65, :], onescol[:],
                                        PT[:, gpt, 512:1024], start=first, stop=last,
                                        tile_position=(0, 64), skip_group_check=True)
                    pv_done[(t, pk)] = st('pe', pvmm, waits=pv_waits, inc=('s_pe_pv', 1))

                if j == 1:
                    emit_norm_tail()
                    emit_norm_tail()

            for (t, nkc) in streams:
                opar = t % 2
                st('act', lambda e, opar=opar: e.activation(
                    rs_sb[0:1, opar, :], ps_den[opar][0:1, :], AF.Ln),
                    waits=[('s_pe_pv', pv_done[(t, nkc - 1)]),
                           ('s_pe_user', last_bcast_user[opar])],
                    inc=('s_chna', 1))
                ln1 = st('act', lambda e, opar=opar: e.activation(
                    rs_sb[64:65, opar, :], ps_den[opar][64:65, :], AF.Ln),
                    inc=('s_chna', 1))
                st('act', lambda e, opar=opar: e.activation(
                    rs_sb[0:1, opar, :], rs_sb[0:1, opar, :], AF.Exp, scale=-1.0),
                    waits=[('s_chna', ln1)], inc=('s_dve_rs', 1))
                rs1 = st('act', lambda e, opar=opar: e.activation(
                    rs_sb[64:65, opar, :], rs_sb[64:65, opar, :], AF.Exp, scale=-1.0),
                    inc=('s_dve_rs', 1))
                last_rs[opar] = rs1
                pending_norm.append((opar, p, t, rs1))

        kq_users_end[p] = mk.c['s_pe_user']
        pv_end_pair[p] = mk.c['s_pe_pv']

    emit_norm_tail()
    emit_norm_tail()
    on_final = mk.c['s_dve_on']
    plc_final = [('s_plc0', mk.c.get('s_plc0', 0)), ('s_plc1', mk.c.get('s_plc1', 0))]
    chna_p1 = mk.c.get('s_chna', 0)
    p1_dead = [('s_pe_pv', mk.c['s_pe_pv']), ('s_pe_user', mk.c['s_pe_user']),
               ('s_dve_on', on_final), ('s_mskd', mk.c.get('s_mskd', 0))] + plc_final
    _w1 = [('s_dve_on', on_final)]
    mk.tap('K2T', K2T[:], (128, SEQ), BF16, waits=_w1)
    mk.tap('Q2T', Q2T[:], (128, NQ), BF16, waits=_w1)
    mk.tap('V2', V2[:], (128, NTC, 256), BF16, waits=_w1)
    mk.tap('o_allT', o_allT[:], (128, NPAIR, NQ), BF16, waits=_w1)
    s_p1.close()
    s_xt.close()
    s_mk.close()

    mrow = SB('mrow', (1, 2, 512), F32, 'right')
    vrow = SB('vrow', (1, 2, 512), F32, 'right')
    srow = SB('srow', (1, 2, 512), F32, 'right')
    rrow = SB('rrow', (1, 2, 512), F32, 'right')
    wd_sb = SB('wd_sb', (128, NHC, NDC, 128), BF16, 'right')
    wd_snap = 0
    s_p2 = ExitStack()
    y1 = s_p2.enter_context(nc.sbuf_tensor('y1', [128, 2, NEC, 512], F32))
    wo_sb = s_p2.enter_context(nc.sbuf_tensor('wo_sb', [128, NEC, NPAIR, 128], BF16))
    yb_sb = s_p2.enter_context(nc.sbuf_tensor('yb_sb', [128, 2, 512], BF16))
    sq_sb = s_p2.enter_context(nc.sbuf_tensor('sq_sb', [128, 2, 512], BF16))
    tmp_sb = s_p2.enter_context(nc.sbuf_tensor('tmp_sb', [128, 512], F32))
    po = [s_p2.enter_context(nc.psum_tensor(f'po{i}', [128, 512], F32)) for i in range(4)]
    pbc = [s_p2.enter_context(nc.psum_tensor(f'pbc{i}', [128, 512], F32)) for i in range(2)]
    pstat_s = s_p2.enter_context(nc.psum_tensor('pstat_s', [1, 512], F32))
    pstat_q = s_p2.enter_context(nc.psum_tensor('pstat_q', [1, 512], F32))

    wo_user = [0, 0, 0, 0]
    po_free = [('s_dve2', 0)] * 4
    yb_free = [0, 0]
    y1_free = 0
    rows_chn_p = 0
    bcast_read = 0
    apply_chn = 0

    for ec in range(NEC):
        st('pool', lambda e, ec=ec: e.dma_start(wo_sb[:, ec], d['wo'][ec]),
           waits=([('s_pe_user', xT_last_user)] if ec == 0 else []),
           inc=('s_ldw2', 16))
    wo_total = mk.c['s_ldw2']
    st('pool', lambda e: e.nop(), waits=[('s_ldw2', wo_total)])

    evacs = {}
    p2snap = {}
    y1_free = [0, 0]
    for qt in range(5):
      qp = 1 - qt % 2
      if qt < 4:
        for ec in range(NEC):
            par = ec % 4

            def omm(e, ec=ec, par=par, qt=qt):
                inst = None
                for hdc in range(NPAIR):
                    inst = e.matmul(po[par][:], wo_sb[:, ec, hdc, :],
                                    o_allT[:, hdc, qt * 512:(qt + 1) * 512],
                                    start=(hdc == 0), stop=(hdc == NPAIR - 1))
                return inst
            ow = [('s_ldw2', wo_total), po_free[par]]
            if qt == 0 and ec == 0:
                ow += [('s_dve_on', on_final)] + plc_final
            wo_user[par] = st('pe', omm, waits=ow, inc=('s_pe2', 1))
            evacs[(qt, ec)] = st('dve', lambda e, ec=ec, par=par, qt=qt, qp=qp:
                e.scalar_tensor_tensor(
                    y1[:, qp, ec, :], po[par][:], bo_sb[:, ec:ec + 1],
                    xqT[:, ec, qt * 512:(qt + 1) * 512], ALU.add, ALU.add),
                waits=[('s_pe2', wo_user[par]), ('s_act2', y1_free[qp])],
                inc=('s_dve2', 1))
            po_free[par] = ('s_dve2', evacs[(qt, ec)])

        wd_snap = st('pool', lambda e, ci=qt: e.dma_start(
            wd_sb[:, ci * 6:(ci + 1) * 6], d['wd'][:, ci * 6:(ci + 1) * 6]),
            waits=(p1_dead if qt == 0 else []),
            inc=('s_ldw3', 16))
        st('pool', lambda e: e.nop(), waits=[('s_ldw3', wd_snap)])

      if qt >= 1:
        lq = qt - 1
        lp = 1 - lq % 2
        for ec in range(NEC):
            par = ec % 2
            st('act', lambda e, ec=ec, par=par, lp=lp: e.activation(
                yb_sb[:, par, :], y1[:, lp, ec, :], AF.Copy),
                waits=[('s_dve2', evacs[(lq, ec)]), ('s_pe2st', yb_free[par])],
                inc=('s_act2', 1))
            asnap = st('act', lambda e, ec=ec, par=par, lp=lp: e.activation(
                sq_sb[:, par, :], y1[:, lp, ec, :], AF.Square),
                inc=('s_act2', 1))

            def stmm(e, ec=ec, par=par):
                e.matmul(pstat_s[:], onescol[:], yb_sb[:, par, :],
                         start=(ec == 0), stop=(ec == NEC - 1))
                return e.matmul(pstat_q[:], onescol[:], sq_sb[:, par, :],
                                start=(ec == 0), stop=(ec == NEC - 1))
            w = [('s_act2', asnap)]
            if ec == 0:
                w.append(('s_chnd', rows_chn_p))
                if lq == 0:
                    w.append(('s_chna', chna_p1))
            yb_free[par] = st('pe', stmm, waits=w, inc=('s_pe2st', 1))
        y1_free[lp] = mk.c['s_act2']

        c1 = st('dve', lambda e: e.tensor_scalar(mrow[:, 0, :], pstat_s[:], 1.0 / EMB,
                                                 None, ALU.mult),
                waits=[('s_pe2st', yb_free[(NEC - 1) % 2]), ('s_pe2bc', bcast_read),
                       ('s_chnd', apply_chn)],
                inc=('s_chnd', 1))
        st('dve', lambda e: e.tensor_scalar(vrow[:, 0, :], pstat_q[:], 1.0 / EMB,
                                            None, ALU.mult), inc=('s_chnd', 1))
        c3 = st('dve', lambda e: e.tensor_mul(srow[:, 0, :], mrow[:, 0, :], mrow[:, 0, :]),
                waits=[('s_chnd', c1)], inc=('s_chnd', 1))
        c4 = st('dve', lambda e: e.tensor_sub(vrow[:, 0, :], vrow[:, 0, :], srow[:, 0, :]),
                waits=[('s_chnd', c3)], inc=('s_chnd', 1))
        rows_chn_p = st('dve', lambda e: e.tensor_scalar(vrow[:, 0, :], vrow[:, 0, :], 1e-5,
                                                         None, ALU.add),
                        waits=[('s_chnd', c4)], inc=('s_chnd', 1))
        sq_snap = st('act', lambda e: e.activation(srow[:, 0, :], vrow[:, 0, :], AF.Ln),
                     waits=[('s_chnd', rows_chn_p)], inc=('s_act2', 1))
        rr_chn = st('act', lambda e: e.activation(rrow[:, 0, :], srow[:, 0, :], AF.Exp,
                                                  scale=-0.5),
                    waits=[('s_act2', sq_snap)], inc=('s_chna', 1))

        def bc2(e):
            e.matmul(pbc[0][:], onesrf[0:1, :], mrow[:, 0, :], start=True, stop=True)
            return e.matmul(pbc[1][:], onesrf[0:1, :], rrow[:, 0, :], start=True, stop=True)
        bcast_read = st('pe', bc2,
                        waits=[('s_chna', rr_chn), ('s_chnd', apply_chn)],
                        inc=('s_pe2bc', 1))

        prev = 0
        for ec in range(NEC):
            sb = st('dve', lambda e, ec=ec, lp=lp: e.tensor_sub(
                tmp_sb[:], y1[:, lp, ec, :], pbc[0][:]),
                    waits=[('s_pe2bc', bcast_read), ('s_chnd', prev)],
                    inc=('s_chnd', 1))
            prev = st('dve', lambda e, ec=ec, lq=lq: e.tensor_mul(
                x1T[:, ec, lq * 512:(lq + 1) * 512], tmp_sb[:], pbc[1][:]),
                waits=[('s_chnd', sb)], inc=('s_chnd', 1))
        apply_chn = prev
        p2snap[lq] = (apply_chn, y1_free[0], y1_free[1])

    omm_end = mk.c['s_pe2']
    dve2_final = mk.c['s_dve2']
    pe2st_final = mk.c['s_pe2st']
    act2_final = mk.c['s_act2']
    chna_final = mk.c['s_chna']
    bcast_final = bcast_read
    apply_final = apply_chn
    mk.tap('x1T', x1T[:], (128, NPAIR, NQ), BF16,
           waits=[('s_chnd', mk.c.get('s_chnd', 0)), ('s_chna', mk.c.get('s_chna', 0)), ('s_dve2', mk.c['s_dve2'])])
    s_p2.close()
    s_xq.close()

    s_p3 = ExitStack()
    hT = s_p3.enter_context(nc.sbuf_tensor('hT', [128, NHC, 1024], BF16))
    wgb = s_p3.enter_context(nc.sbuf_tensor('wgb', [128, 4, NEC, 128], BF16))
    wub = s_p3.enter_context(nc.sbuf_tensor('wub', [128, 4, NEC, 128], BF16))
    gs_sb = s_p3.enter_context(nc.sbuf_tensor('gs_sb', [128, 2, 512], BF16))
    y2 = s_p3.enter_context(nc.sbuf_tensor('y2', [128, NEC, 1024], F32))
    yb3 = s_p3.enter_context(nc.sbuf_tensor('yb3', [128, 2, 512], BF16))
    sq3 = s_p3.enter_context(nc.sbuf_tensor('sq3', [128, 2, 512], BF16))
    tmp3 = s_p3.enter_context(nc.sbuf_tensor('tmp3', [128, 512], F32))
    pA = s_p3.enter_context(nc.psum_tensor('pA', [128, 512], F32))
    pB = s_p3.enter_context(nc.psum_tensor('pB', [128, 512], F32))
    pC = s_p3.enter_context(nc.psum_tensor('pC', [128, 512], F32))
    pD = s_p3.enter_context(nc.psum_tensor('pD', [128, 512], F32))
    pst = [[s_p3.enter_context(nc.psum_tensor(f'pst{h}{i}', [128, 512], F32))
            for i in range(2)] for h in range(2)]
    pg = [pA, pB]
    pu = [pC, pD]
    pd = [pA, pB]


    wgu_snap = {}
    wgu_free = [0, 0, 0, 0]
    silu_free = [0, 0]
    hm_free = [0, 0]
    down_end = 0
    y2ev_free = [0, 0]
    stat_free = [0, 0]
    yb3_free = [0, 0]
    y2_free = 0
    apply3 = 0
    bc3_read = 0
    st_dma = 0

    def load_wgu(qt2, s):
        h0, h1 = 2 * s, 2 * s + 1
        gate = max(wgu_free[h0 % 4], wgu_free[h1 % 4])
        w0 = [('s_pe3g', gate)]
        if qt2 == 0 and s == 0:
            w0 += [('s_act2', p2snap[2][2]), ('s_chnd', p2snap[2][0])]
        st('pool', lambda e, hc=h0: e.dma_start(wgb[:, hc % 4], d['wg'][hc]),
           waits=w0, inc=('s_ldw3', 16))
        st('pool', lambda e, hc=h0: e.dma_start(wub[:, hc % 4], d['wu'][hc]),
           inc=('s_ldw3', 16))
        st('pool', lambda e, hc=h1: e.dma_start(wgb[:, hc % 4], d['wg'][hc]),
           inc=('s_ldw3', 16))
        snap = st('pool', lambda e, hc=h1: e.dma_start(wub[:, hc % 4], d['wu'][hc]),
                  inc=('s_ldw3', 16))
        wgu_snap[(qt2, h0)] = wgu_snap[(qt2, h1)] = snap
        st('pool', lambda e: e.nop(), waits=[('s_ldw3', snap)])

    for qt2 in range(2):
        q0 = qt2 * 1024
        load_wgu(qt2, 0)
        load_wgu(qt2, 1)
        hmults = []
        for hc in range(NHC):
            par = hc % 2
            for half in range(2):
                w = 2 * hc + half
                wp = w % 2
                q1 = q0 + half * 512

                def gmm(e, par=hc % 4, wp=wp, q1=q1):
                    inst = None
                    for ec in range(NEC):
                        inst = e.matmul(pg[wp][:], wgb[:, par, ec, :],
                                        x1T[:, ec, q1:q1 + 512],
                                        start=(ec == 0), stop=(ec == NEC - 1))
                    return inst
                gw = [('s_ldw3', wgu_snap[(qt2, hc)]),
                      ('s_act3', silu_free[wp]), ('s_pe3d', down_end),
                      ('s_dve3', y2ev_free[wp])]
                if hc == 0 and half == 0:
                    gw += [('s_chnd', p2snap[2 * qt2 + 1][0]),
                           ('s_dve2', dve2_final)]
                gsnap = st('pe', gmm, waits=gw, inc=('s_pe3g', 1))

                def umm(e, par=hc % 4, wp=wp, q1=q1):
                    inst = None
                    for ec in range(NEC):
                        inst = e.matmul(pu[wp][:], wub[:, par, ec, :],
                                        x1T[:, ec, q1:q1 + 512],
                                        start=(ec == 0), stop=(ec == NEC - 1))
                    return inst
                uw = [('s_dve3', hm_free[wp])]
                if qt2 == 0 and hc == 0 and half == 0:
                    uw += [('s_dve2', dve2_final)]
                usnap = st('pe', umm, waits=uw, inc=('s_pe3g', 1))
                slw = [('s_pe3g', gsnap), ('s_dve3', hm_free[wp])]
                if qt2 == 0 and hc == 0 and half == 0:
                    slw += [('s_chnd', p2snap[2][0])]
                silu_free[wp] = st('act', lambda e, wp=wp: e.activation(
                    gs_sb[:, wp, :], pg[wp][:], AF.Silu, scale=1.0 / 127.0),
                    waits=slw, inc=('s_act3', 1))
                hw = [('s_pe3g', usnap), ('s_act3', silu_free[wp])]
                if qt2 == 0 and hc == 16 and half == 0:
                    hw += [('s_act2', p2snap[3][1])]
                hm_free[wp] = st('dve', lambda e, wp=wp, hc=hc, half=half: e.tensor_mul(
                    hT[:, hc, half * 512:(half + 1) * 512], pu[wp][:], gs_sb[:, wp, :]),
                    waits=hw, inc=('s_dve3', 1))
                hmults.append(hm_free[wp])
            wgu_free[hc % 4] = mk.c['s_pe3g']
            if hc % 2 == 1 and hc + 3 < NHC:
                load_wgu(qt2, (hc + 3) // 2)
        hT_ready = hmults[-1]

        ev3 = {}
        ybsq = {}
        for half in range(2):
            for dc in range(NDC):
                dw = half * NDC + dc
                dp = dw % 2
                q1 = q0 + half * 512

                def dmm(e, dc=dc, half=half, dp=dp):
                    inst = None
                    for hc in range(NHC):
                        inst = e.matmul(pd[dp][:], wd_sb[:, hc, dc, :],
                                        hT[:, hc, half * 512:(half + 1) * 512],
                                        start=(hc == 0), stop=(hc == NHC - 1))
                    return inst
                w = [('s_dve3', hT_ready), ('s_ldw3', wd_snap),
                     ('s_dve3', y2ev_free[dp]), ('s_act3', silu_free[dp])]
                dsnap = st('pe', dmm, waits=w, inc=('s_pe3d', 1))
                ew = [('s_pe3d', dsnap), ('s_out', st_dma), ('s_act3', y2_free)]
                if qt2 == 0 and half == 0 and dc == 0:
                    ew += [('s_act2', p2snap[2][2]), ('s_pe2st', pe2st_final)]
                ev3[(dc, half)] = st('dve', lambda e, dc=dc, half=half, dp=dp, q1=q1:
                    e.scalar_tensor_tensor(
                        y2[:, dc, half * 512:(half + 1) * 512], pd[dp][:], FS2,
                        x1T[:, dc, q1:q1 + 512], ALU.mult, ALU.add),
                    waits=ew, inc=('s_dve3', 1))
                y2ev_free[dp] = ev3[(dc, half)]
        down_end = mk.c['s_pe3d']

        for half in range(2):
            for dc in range(NDC):
                par = dc % 2
                st('act', lambda e, dc=dc, par=par, half=half: e.activation(
                    yb3[:, par, :], y2[:, dc, half * 512:(half + 1) * 512], AF.Copy),
                    waits=[('s_dve3', ev3[(dc, half)]), ('s_pe3st', yb3_free[par])],
                    inc=('s_act3', 1))
                asnap = st('act', lambda e, dc=dc, par=par, half=half: e.activation(
                    sq3[:, par, :], y2[:, dc, half * 512:(half + 1) * 512], AF.Square),
                    inc=('s_act3', 1))

                def stmm3(e, dc=dc, par=par, half=half):
                    e.matmul(pst[half][0][0:1, :], onescol[:], yb3[:, par, :],
                             start=(dc == 0), stop=(dc == NDC - 1))
                    return e.matmul(pst[half][1][0:1, :], onescol[:], sq3[:, par, :],
                                    start=(dc == 0), stop=(dc == NDC - 1))
                w = [('s_act3', asnap)]
                if dc == 0:
                    w += [('s_chnd', stat_free[half]), ('s_dve3', hm_free[0]),
                          ('s_dve3', hm_free[1]), ('s_chnd', apply3)]
                    if qt2 == 0 and half == 0:
                        w += [('s_chnd', apply_final)]
                yb3_free[par] = st('pe', stmm3, waits=w, inc=('s_pe3st', 1))
            ybsq[half] = mk.c['s_pe3st']
        y2_free = mk.c['s_act3']

        rr3h = [0, 0]
        for half in range(2):
            cw = [('s_pe3st', ybsq[half]), ('s_pe3bc', bc3_read)]
            if qt2 == 0 and half == 0:
                cw += [('s_pe2bc', bcast_final), ('s_chna', chna_final),
                       ('s_act2', act2_final)]
            c1 = st('dve', lambda e, half=half: e.tensor_scalar(
                mrow[:, half, :], pst[half][0][0:1, :], 1.0 / EMB, None, ALU.mult),
                    waits=cw, inc=('s_chnd', 1))
            st('dve', lambda e, half=half: e.tensor_scalar(
                vrow[:, half, :], pst[half][1][0:1, :], 1.0 / EMB, None, ALU.mult),
               inc=('s_chnd', 1))
            c3 = st('dve', lambda e, half=half: e.tensor_mul(
                srow[:, half, :], mrow[:, half, :], mrow[:, half, :]),
                    waits=[('s_chnd', c1)], inc=('s_chnd', 1))
            c4 = st('dve', lambda e, half=half: e.tensor_sub(
                vrow[:, half, :], vrow[:, half, :], srow[:, half, :]),
                    waits=[('s_chnd', c3)], inc=('s_chnd', 1))
            stat_free[half] = st('dve', lambda e, half=half: e.tensor_scalar(
                vrow[:, half, :], vrow[:, half, :], 1e-5, None, ALU.add),
                           waits=[('s_chnd', c4)], inc=('s_chnd', 1))
            sq_snap3 = st('act', lambda e, half=half: e.activation(
                srow[:, half, :], vrow[:, half, :], AF.Ln),
                          waits=[('s_chnd', stat_free[half])], inc=('s_act3', 1))
            rr3h[half] = st('act', lambda e, half=half: e.activation(
                rrow[:, half, :], srow[:, half, :], AF.Exp, scale=-0.5),
                     waits=[('s_act3', sq_snap3)], inc=('s_chna', 1))

        for half in range(2):
            q1 = q0 + half * 512
            bcb = [pst[1][0], pst[1][1]] if half == 0 else [pst[0][0], pst[0][1]]

            def bc3f(e, half=half, bcb=bcb):
                e.matmul(bcb[0][:], onesrf[0:1, :], mrow[:, half, :],
                         start=True, stop=True, skip_group_check=True)
                return e.matmul(bcb[1][:], onesrf[0:1, :], rrow[:, half, :],
                                start=True, stop=True, skip_group_check=True)
            bw = [('s_chna', rr3h[half])]
            if half == 0:
                bw += [('s_chnd', stat_free[1])]
            else:
                bw += [('s_chnd', stat_free[0])]
            bc3_read = st('pe', bc3f, waits=bw, inc=('s_pe3bc', 1))

            prev3 = apply3
            for dc in range(NDC):
                sb3 = st('dve', lambda e, dc=dc, half=half, bcb=bcb: e.tensor_sub(
                    tmp3[:], y2[:, dc, half * 512:(half + 1) * 512], bcb[0][:]),
                   waits=[('s_pe3bc', bc3_read), ('s_chnd', prev3)],
                   inc=('s_chnd', 1))
                prev3 = st('dve', lambda e, dc=dc, half=half, bcb=bcb: e.tensor_mul(
                    y2[:, dc, half * 512:(half + 1) * 512], tmp3[:], bcb[1][:]),
                    waits=[('s_chnd', sb3)], inc=('s_chnd', 1))
            apply3 = prev3
            ydst = y.rearrange("(ec p) q -> p ec q", p=128)[:, :, q1:q1 + 512]
            st_dma = st('sp', lambda e, ydst=ydst, half=half: e.dma_start(
                ydst, y2[:, :, half * 512:(half + 1) * 512]),
                waits=[('s_chnd', apply3)], inc=('s_out', 16))
            st('sp', lambda e: e.nop(), waits=[('s_out', st_dma)])

    st('sp', lambda e: e.nop(), waits=[('s_out', mk.c['s_out'])])
    _w3 = [('s_chnd', mk.c.get('s_chnd', 0)), ('s_chna', mk.c.get('s_chna', 0)), ('s_dve3', mk.c['s_dve3'])]
    mk.tap('hT', hT[:], (128, NHC, 1024), BF16, waits=_w3)
    mk.tap('y2', y2[:], (128, NEC, 1024), F32, waits=_w3)
    mk.flush()
    s_p3.close()
    mk.es.close()
    return mk



def _prep_shared(inp):
    bf = ml_dtypes.bfloat16
    Wq = np.asarray(inp['L0_attn_Wq'], np.float32)
    Wk = np.asarray(inp['L0_attn_Wk'], np.float32)
    Wv = np.asarray(inp['L0_attn_Wv'], np.float32)
    Wo = np.asarray(inp['L0_attn_Wo'], np.float32)
    bq = np.asarray(inp['L0_attn_bq'], np.float32)
    bk = np.asarray(inp['L0_attn_bk'], np.float32)
    bv = np.asarray(inp['L0_attn_bv'], np.float32)
    bo = np.asarray(inp['L0_attn_bo'], np.float32)
    wg = np.asarray(inp['L0_ffn_w_gate'], np.int32)
    wu = np.asarray(inp['L0_ffn_w_up'], np.int32)
    wd = np.asarray(inp['L0_ffn_w_down'], np.int32)

    def qk_tiles(W):
        fm = np.ascontiguousarray(np.transpose(W, (1, 0, 2)).reshape(EMB, EMB))
        out = np.empty((NPAIR, 128, NEC, 128), np.float32)
        for p in range(NPAIR):
            blk = fm[:, 128 * p:128 * (p + 1)]
            out[p] = blk.reshape(NEC, 128, 128).transpose(1, 0, 2)
        return np.ascontiguousarray(out)

    sh = {}
    sh['wq'] = qk_tiles(Wq)
    sh['wk'] = qk_tiles(Wk)
    vfm = np.transpose(Wv, (1, 0, 2)).reshape(EMB, EMB)
    v1 = vfm.reshape(NEC, 128, EMB).transpose(1, 0, 2)
    sh['wv'] = np.ascontiguousarray(
        v1.reshape(128, NEC, 4, 256).transpose(2, 0, 1, 3))
    sh['wo'] = np.ascontiguousarray(
        Wo.reshape(NPAIR, 128, NEC, 128).transpose(2, 1, 0, 3))
    sh['bq'] = np.ascontiguousarray(bq.reshape(NPAIR, 128).T)
    sh['bk'] = np.ascontiguousarray(bk.reshape(NPAIR, 128).T)
    sh['bo'] = np.ascontiguousarray(bo.reshape(NEC, 128).T)
    sh['bv'] = np.ascontiguousarray(bv.reshape(1, EMB))
    sh['wg'] = np.ascontiguousarray(
        wg.reshape(NHC, 128, NEC, 128).transpose(0, 3, 2, 1))
    sh['wu'] = np.ascontiguousarray(
        wu.reshape(NHC, 128, NEC, 128).transpose(0, 3, 2, 1))
    sh['wd'] = np.ascontiguousarray(
        wd.reshape(NDC, 128, NHC, 128).transpose(3, 2, 0, 1))
    sh['ident'] = np.eye(128, dtype=bf)
    sh['ones_col'] = np.ones((128, 1), bf)
    sh['ones_row'] = np.ones((128, 128), bf)
    sh['ones_rf'] = np.ones((128, 128), np.float32)
    return sh


def _masks_for_role(role):
    out = np.zeros((4, 8, 128, 512), np.uint8)
    ii = np.arange(128)[:, None]
    jj = np.arange(512)[None, :]
    for t in range(4):
        q0 = 512 * TILES[role][t]
        for rel in range(8):
            kc = CT[t] - 8 + rel
            out[t, rel] = ((128 * kc + ii) <= (q0 + jj)).astype(np.uint8)
    return np.ascontiguousarray(out.transpose(2, 0, 1, 3))


def kernel(**inputs):
    inp = {k: np.asarray(v) for k, v in inputs.items()}
    x = np.asarray(inp['L0_attn_x'], np.float32)
    sh = _prep_shared(inp)
    mk_role = [_masks_for_role(0), _masks_for_role(1)]

    in_maps = []
    for c in range(8):
        s, r = c // 2, c % 2
        xs = np.ascontiguousarray(x[s])
        xq = np.ascontiguousarray(
            np.concatenate([xs[512 * j:512 * (j + 1)] for j in TILES[r]], axis=0))
        m = dict(sh)
        m['x'] = xs
        m['xq'] = xq
        m['mk'] = mk_role[r]
        in_maps.append(m)

    mk = build_full()
    res = run_bass_kernel_spmd(mk.nc, in_maps, core_ids=list(range(8)))

    out = np.empty((4, SEQ, EMB), np.float32)
    for c in range(8):
        s, r = c // 2, c % 2
        yT = np.asarray(res.results[c]['y']).T
        for t, j in enumerate(TILES[r]):
            out[s, 512 * j:512 * (j + 1)] = yT[512 * t:512 * (t + 1)]
    return out


# version 39
# speedup vs baseline: 1.0042x; 1.0042x over previous
import numpy as np
import ml_dtypes
from contextlib import ExitStack

import concourse.bass as bass
import concourse.mybir as mybir
from concourse.bass_utils import run_bass_kernel_spmd

F32 = mybir.dt.float32
BF16 = mybir.dt.bfloat16
I32 = mybir.dt.int32
U8 = mybir.dt.uint8
AF = mybir.ActivationFunctionType
ALU = mybir.AluOpType

SEQ, EMB, NHEAD, DHEAD, HID = 4096, 1024, 16, 64, 3072
NPAIR, NEC, NHC, NDC = 8, 8, 24, 8
NQ = 2048
CT = (8, 16, 24, 32)
TILES = ((0, 2, 5, 7), (1, 3, 4, 6))
NTC = SEQ // 128
SCALE = 0.125
FS2 = 1.0 / (127.0 * 127.0)


class MK:
    def __init__(self, taps=()):
        self.taps = set(taps)
        self.nc = bass.Bass("TRN2", target_bir_lowering=False)
        self.es = ExitStack()
        self.sems = {}
        self.c = {}
        self.steps = []
        self.lw = {}
        self.tap_outs = {}

    def sem(self, name):
        if name not in self.sems:
            self.sems[name] = self.es.enter_context(self.nc.semaphore(name))
            self.c[name] = 0
        return self.sems[name]

    def st(self, eng, fn, waits=(), inc=None):
        ws = []
        for s, v in waits:
            if v is not None and v > 0:
                self.sem(s)
                ws.append((s, v))
        if inc:
            self.sem(inc[0])
        self.steps.append((eng, tuple(ws), fn, inc))
        if inc:
            self.c[inc[0]] += inc[1]
            return self.c[inc[0]]
        return None

    def flush(self):
        steps, self.steps = self.steps, []
        nc, sems, lw = self.nc, self.sems, self.lw

        def runner(engname):
            def body(eng):
                for en, ws, fn, inc in steps:
                    if en != engname:
                        continue
                    for s, v in ws:
                        k = (en, s)
                        if lw.get(k, 0) >= v:
                            continue
                        eng.wait_ge(sems[s], v)
                        lw[k] = v
                    inst = fn(eng)
                    if inc is not None:
                        inst.then_inc(sems[inc[0]], inc[1])
            return body

        with nc.Block() as blk:
            blk.sync(runner('sp'))
            blk.scalar(runner('act'))
            blk.tensor(runner('pe'))
            blk.vector(runner('dve'))
            blk.gpsimd(runner('pool'))

    def tap(self, name, ap, shape, dt, waits=()):
        if name not in self.taps:
            return
        t = self.nc.dram_tensor('dbg_' + name, list(shape), dt, kind="ExternalOutput").ap()
        self.tap_outs[name] = t
        cnt = self.st('sp', lambda e, t=t, ap=ap: e.dma_start(t, ap), waits=waits,
                      inc=('s_tap', 16))
        self.st('sp', lambda e: e.nop(), waits=[('s_tap', cnt)])


def build_full(taps=()):
    mk = MK(taps)
    nc = mk.nc
    st = mk.st

    d = {}
    for name, shape, dt in [
        ('x', (SEQ, EMB), F32), ('xq', (NQ, EMB), F32),
        ('wq', (NPAIR, 128, NEC, 128), F32), ('wk', (NPAIR, 128, NEC, 128), F32),
        ('wv', (4, 128, NEC, 256), F32), ('wo', (NEC, 128, NPAIR, 128), F32),
        ('bq', (128, NPAIR), F32), ('bk', (128, NPAIR), F32),
        ('bo', (128, NEC), F32), ('bv', (1, EMB), F32),
        ('wg', (NHC, 128, NEC, 128), I32), ('wu', (NHC, 128, NEC, 128), I32),
        ('wd', (128, NHC, NDC, 128), I32),
        ('mk', (128, 4, 8, 512), U8),
        ('ident', (128, 128), BF16), ('ones_col', (128, 1), BF16),
        ('ones_row', (128, 128), BF16), ('ones_rf', (128, 128), F32),
    ]:
        d[name] = nc.dram_tensor(name, list(shape), dt, kind="ExternalInput").ap()
    y = nc.dram_tensor('y', [EMB, NQ], F32, kind="ExternalOutput").ap()

    SB = lambda n, sh, dt, side='left': mk.es.enter_context(
        nc.sbuf_tensor(n, list(sh), dt, side=side))
    ident_sb = SB('ident_sb', (128, 128), BF16, 'right')
    onescol = SB('onescol', (128, 1), BF16, 'right')
    onesrow = SB('onesrow', (128, 128), BF16, 'right')
    onesrf = SB('onesrf', (128, 128), F32, 'right')
    bq_sb = SB('bq_sb', (128, NPAIR), F32, 'right')
    bk_sb = SB('bk_sb', (128, NPAIR), F32, 'right')
    bo_sb = SB('bo_sb', (128, NEC), F32, 'right')
    bv_bf = SB('bv_bf', (1, EMB), BF16, 'right')

    o_allT = SB('o_allT', (128, NPAIR, NQ), BF16)
    x1T = o_allT
    s_xq = ExitStack()
    xqT = s_xq.enter_context(nc.sbuf_tensor('xqT', [128, NEC, NQ], BF16))
    s_mk = ExitStack()
    masks_sb = s_mk.enter_context(nc.sbuf_tensor('masks_sb', [128, 4, 8, 512], U8))
    s_xt = ExitStack()
    xT = s_xt.enter_context(nc.sbuf_tensor('xT', [128, NEC, SEQ], BF16))

    s_p0 = ExitStack()
    stg = s_p0.enter_context(nc.sbuf_tensor('stg', [128, 16, 1024], BF16))
    pt_ps = [s_p0.enter_context(nc.psum_tensor(f'pt{i}', [128, NEC, 128], BF16))
             for i in range(2)]

    for nm, dst in [('ident', ident_sb), ('ones_col', onescol), ('ones_row', onesrow),
                    ('ones_rf', onesrf), ('bq', bq_sb), ('bk', bk_sb), ('bo', bo_sb),
                    ('mk', masks_sb)]:
        st('sp', lambda e, s=d[nm], t=dst: e.dma_start(t[:], s), inc=('s_cst', 16))
    cst_total = mk.c['s_cst']
    st('sp', lambda e: e.nop(), waits=[('s_cst', cst_total)])
    st('pool', lambda e: e.dma_start(bv_bf[:], d['bv']), inc=('s_dma0', 16))
    st('pool', lambda e: e.nop(), waits=[('s_dma0', mk.c['s_dma0'])])

    wave_total = {}
    for w in range(3):
        for i in range(16 * w, 16 * w + 16):
            if i < 32:
                src = d['x'][i * 128:(i + 1) * 128, :]
            else:
                src = d['xq'][(i - 32) * 128:(i - 31) * 128, :]
            st('pool', lambda e, s=src, sl=i % 16: e.dma_start(stg[:, sl, :], s),
               waits=[('s_dve0', i - 15)], inc=('s_dma0', 16))
        wave_total[w] = mk.c['s_dma0']
        st('pool', lambda e: e.nop(), waits=[('s_dma0', wave_total[w])])

    for i in range(48):
        par = i % 2
        if i < 32:
            cp = (lambda e, tc=i, p=par: e.tensor_copy(
                xT[:, :, tc * 128:(tc + 1) * 128], pt_ps[p][:]))
        else:
            cp = (lambda e, tc=i - 32, p=par: e.tensor_copy(
                xqT[:, :, tc * 128:(tc + 1) * 128], pt_ps[p][:]))

        def trn(e, p=par, sl=i % 16):
            inst = None
            for ec in range(NEC):
                inst = e.transpose(pt_ps[p][:, ec, :],
                                   stg[:, sl, ec * 128:(ec + 1) * 128], ident_sb[:])
            return inst
        st('pe', trn, waits=[('s_cst', cst_total), ('s_dma0', wave_total[i // 16]),
                             ('s_dve0', i - 1)],
           inc=('s_pe0', 1))
        st('dve', cp, waits=[('s_pe0', i + 1)], inc=('s_dve0', 1))

    mk.tap('xT', xT[:], (128, NEC, SEQ), BF16, waits=[('s_dve0', 48)])
    mk.tap('xqT', xqT[:], (128, NEC, NQ), BF16, waits=[('s_dve0', 48)])
    s_p0.close()

    s_p1 = ExitStack()
    V2 = s_p1.enter_context(nc.sbuf_tensor('V2', [128, NTC, 256], BF16))
    K2T = s_p1.enter_context(nc.sbuf_tensor('K2T', [128, SEQ], BF16))
    Q2T = s_p1.enter_context(nc.sbuf_tensor('Q2T', [128, NQ], BF16))
    PT = s_p1.enter_context(nc.sbuf_tensor('PT', [128, 5, 1024], BF16))
    rb2 = s_p1.enter_context(nc.sbuf_tensor('rb2', [128, 1024], BF16))
    rs_sb = s_p1.enter_context(nc.sbuf_tensor('rs_sb', [65, 2, 512], F32))
    wq_sb = s_p1.enter_context(nc.sbuf_tensor('wq_sb', [128, 2, NEC, 128], BF16))
    wk_sb = s_p1.enter_context(nc.sbuf_tensor('wk_sb', [128, 2, NEC, 128], BF16))
    wv_sb = s_p1.enter_context(nc.sbuf_tensor('wv_sb', [128, 2, NEC, 256], BF16))
    pp = [s_p1.enter_context(nc.psum_tensor(f'pp{i}', [128, 1024], F32)) for i in range(2)]
    ps_o = [s_p1.enter_context(nc.psum_tensor(f'pso{i}', [128, 512], F32))
            for i in range(2)]
    ps_den = [s_p1.enter_context(nc.psum_tensor(f'psd{i}', [65, 512], F32))
              for i in range(2)]

    u = 0
    user_done = []
    pool_cons = []
    g = 0
    pv_snap = {}
    exp_snap = {}
    last_on = [0, 0]
    last_rs = [0, 0]
    exp_list = []
    slot_of = {}
    pv_ready = {}
    pv_done = {}
    last_bcast_user = [0, 0]
    last_rb_on = 0
    kq_users_end = [0] * (NPAIR + 1)
    pv_end_pair = [0] * (NPAIR + 1)
    pending_norm = []

    def emit_norm_tail():
        nonlocal u, last_rb_on
        if not pending_norm:
            return
        opar_, p_, t_, rs1_ = pending_norm.pop(0)
        par = u % 2

        def bcast(e, par=par, opar=opar_):
            e.matmul(pp[par][0:64, 0:512], onesrf[0:1, 0:64], rs_sb[0:1, opar, :],
                     start=True, stop=True)
            return e.matmul(pp[par][64:128, 512:1024], onesrf[64:65, 0:64],
                            rs_sb[64:65, opar, :], start=True, stop=True,
                            tile_position=(64, 64))
        user_done.append(st('pe', bcast,
                            waits=[(f's_plc{u % 2}', u // 2), ('s_dve_rs', rs1_)],
                            inc=('s_pe_user', 1)))
        last_bcast_user[opar_] = user_done[u]
        rb_snap = (f's_plc{u % 2}', st('dve', lambda e, par=par: e.tensor_copy(
            rb2[:], pp[par][:]),
            waits=[('s_pe_user', user_done[u]), ('s_dve_on', last_rb_on)],
            inc=(f's_plc{u % 2}', 1)))
        u += 1

        st('dve', lambda e, opar=opar_, p=p_, t=t_: e.tensor_mul(
            o_allT[0:64, p, t * 512:(t + 1) * 512], ps_o[opar][0:64, :],
            rb2[0:64, 0:512]),
            waits=[rb_snap], inc=('s_dve_on', 1))
        last_on[opar_] = st('dve', lambda e, opar=opar_, p=p_, t=t_: e.tensor_mul(
            o_allT[64:128, p, t * 512:(t + 1) * 512], ps_o[opar][64:128, :],
            rb2[64:128, 512:1024]),
            inc=('s_dve_on', 1))
        last_rb_on = last_on[opar_]
    wqk_free = [0, 0]
    wv_free = [0, 0]
    wqk_snap = [0] * NPAIR
    wv_snap = [0] * 4

    def pool_boundary(semname):
        cnt = mk.c[semname]
        st('pool', lambda e: e.nop(), waits=[(semname, cnt)])

    def load_wqk(p):
        par = p % 2
        gate = wqk_free[par]
        st('pool', lambda e, p=p, par=par: e.dma_start(wq_sb[:, par], d['wq'][p]),
           waits=[('s_pe_user', gate)], inc=('s_ldw', 16))
        wqk_snap[p] = st('pool', lambda e, p=p, par=par: e.dma_start(wk_sb[:, par], d['wk'][p]),
                         inc=('s_ldw', 16))
        pool_boundary('s_ldw')

    def load_wv(gi):
        par = gi % 2
        gate = wv_free[par]
        wv_snap[gi] = st('pool', lambda e, gi=gi, par=par: e.dma_start(wv_sb[:, par], d['wv'][gi]),
                         waits=[('s_pe_user', gate)], inc=('s_ldw', 16))
        pool_boundary('s_ldw')

    load_wqk(0)
    load_wv(0)

    for p in range(NPAIR):
        ppar = p % 2
        if p + 1 < NPAIR:
            load_wqk(p + 1)
        if p % 2 == 1 and (p + 1) // 2 < 4:
            load_wv((p + 1) // 2)

        if p % 2 == 0:
            gi = p // 2
            gpar = gi % 2
            for tc in range(NTC):
                par = u % 2

                def vmm(e, tc=tc, par=par, gpar=gpar, gi=gi):
                    for ec in range(NEC):
                        e.matmul(pp[par][:, 0:256], xT[:, ec, tc * 128:(tc + 1) * 128],
                                 wv_sb[:, gpar, ec, :], start=(ec == 0), stop=False)
                    return e.matmul(pp[par][:, 0:256], onesrow[0:1, :],
                                    bv_bf[0:1, gi * 256:(gi + 1) * 256],
                                    start=False, stop=True)
                vw = [(f's_plc{u % 2}', u // 2), ('s_ldw', wv_snap[gi])]
                if gi == 0 and tc == 0:
                    vw.append(('s_dve0', 48))
                user_done.append(st('pe', vmm, waits=vw, inc=('s_pe_user', 1)))
                pool_cons.append(st('dve', lambda e, tc=tc, par=par: e.tensor_copy(
                    V2[:, tc, :], pp[par][:, 0:256]),
                    waits=[('s_pe_user', user_done[u]),
                           ('s_pe_pv', pv_end_pair[max(p - 1, 0)])],
                    inc=(f's_plc{u % 2}', 1)))
                u += 1
            wv_free[gpar] = user_done[-1]

        for ts in range(8):
            par = u % 2

            def kmm(e, ts=ts, par=par, ppar=ppar):
                inst = None
                for ec in range(NEC):
                    inst = e.matmul(pp[par][:, 0:512], wk_sb[:, ppar, ec, :],
                                    xT[:, ec, ts * 512:(ts + 1) * 512],
                                    start=(ec == 0), stop=(ec == NEC - 1))
                return inst
            user_done.append(st('pe', kmm,
                                waits=[(f's_plc{u % 2}', u // 2), ('s_ldw', wqk_snap[p])],
                                inc=('s_pe_user', 1)))
            pool_cons.append(st('dve', lambda e, ts=ts, par=par, p=p: e.tensor_scalar(
                K2T[:, ts * 512:(ts + 1) * 512], pp[par][:, 0:512],
                bk_sb[:, p:p + 1], None, ALU.add),
                waits=[('s_pe_user', user_done[u]),
                       ('s_pe_user', kq_users_end[max(p - 1, 0)])],
                inc=(f's_plc{u % 2}', 1)))
            u += 1
            if ts in (2, 4):
                emit_norm_tail()
        if p == NPAIR - 1:
            xT_last_user = mk.c['s_pe_user']

        for qs in range(4):
            par = u % 2

            def qmm(e, qs=qs, par=par, ppar=ppar):
                inst = None
                for ec in range(NEC):
                    inst = e.matmul(pp[par][:, 0:512], wq_sb[:, ppar, ec, :],
                                    xqT[:, ec, qs * 512:(qs + 1) * 512],
                                    start=(ec == 0), stop=(ec == NEC - 1))
                return inst
            user_done.append(st('pe', qmm,
                                waits=[(f's_plc{u % 2}', u // 2), ('s_ldw', wqk_snap[p])],
                                inc=('s_pe_user', 1)))
            pool_cons.append(st('dve', lambda e, qs=qs, par=par, p=p: e.tensor_scalar(
                Q2T[:, qs * 512:(qs + 1) * 512], pp[par][:, 0:512],
                bq_sb[:, p:p + 1], None, ALU.add),
                waits=[('s_pe_user', user_done[u]),
                       ('s_pe_user', kq_users_end[max(p - 1, 0)])],
                inc=(f's_plc{u % 2}', 1)))
            u += 1
        wqk_free[ppar] = user_done[-1]
        kq_ready = [('s_plc0', mk.c.get('s_plc0', 0)), ('s_plc1', mk.c.get('s_plc1', 0))]

        NSLOT = 5
        for tp in range(2):
            tA, tB = 2 * tp, 2 * tp + 1
            streams = ((tA, CT[tA]), (tB, CT[tB]))
            nmax = CT[tB]
            for j in range(nmax + 3):
                for (t, nkc) in streams:
                    opar = t % 2
                    if j < nkc:
                        kc = j
                        par = u % 2
                        gpt = g % NSLOT

                        def smm(e, kc=kc, t=t, par=par):
                            e.matmul(pp[par][:, 0:512],
                                     K2T[0:64, kc * 128:(kc + 1) * 128],
                                     Q2T[0:64, t * 512:(t + 1) * 512],
                                     start=True, stop=True)
                            return e.matmul(pp[par][:, 512:1024],
                                            K2T[64:128, kc * 128:(kc + 1) * 128],
                                            Q2T[64:128, t * 512:(t + 1) * 512],
                                            start=True, stop=True, tile_position=(64, 0))
                        user_done.append(st('pe', smm,
                                            waits=[(f's_plc{u % 2}', u // 2)] + kq_ready,
                                            inc=('s_pe_user', 1)))
                        exp_waits = [('s_pe_user', user_done[u])]
                        n_exp = len(exp_list)
                        if n_exp >= NSLOT:
                            exp_waits.append(('s_pe_pv', pv_done[exp_list[n_exp - NSLOT]]))
                        pool_cons.append(st('act', lambda e, par=par, gpt=gpt: e.activation(
                            PT[:, gpt, :], pp[par][:], AF.Exp, scale=SCALE),
                            waits=exp_waits, inc=(f's_plc{u % 2}', 1)))
                        es = (f's_plc{u % 2}', mk.c[f's_plc{u % 2}'])
                        exp_list.append((t, kc))
                        slot_of[(t, kc)] = gpt
                        u += 1
                        g += 1

                        rel = kc - (nkc - 8)
                        if rel >= 0:
                            msnap = st('dve', lambda e, gpt=gpt, t=t, rel=rel: e.tensor_mul(
                                PT[:, gpt, :].rearrange("p (h q) -> p h q", h=2),
                                PT[:, gpt, :].rearrange("p (h q) -> p h q", h=2),
                                masks_sb[:, t, rel, :].unsqueeze(1).broadcast_to([128, 2, 512])),
                                waits=[es], inc=('s_mskd', 1))
                            pv_ready[(t, kc)] = ('s_mskd', msnap)
                        else:
                            pv_ready[(t, kc)] = es

                    pk = j - 2
                    if pk < 0 or pk >= nkc:
                        continue
                    pv_waits = [pv_ready[(t, pk)]]
                    if pk == 0:
                        pv_waits += [('s_dve_on', last_on[opar]),
                                     ('s_dve_rs', last_rs[opar])]

                    def pvmm(e, kc=pk, opar=opar, gpt=slot_of[(t, pk)], ppar=ppar,
                             nkc=nkc):
                        first, last = kc == 0, kc == nkc - 1
                        e.matmul(ps_o[opar][0:64, :],
                                 V2[:, kc, ppar * 128:ppar * 128 + 64],
                                 PT[:, gpt, 0:512], start=first, stop=last)
                        e.matmul(ps_o[opar][64:128, :],
                                 V2[:, kc, ppar * 128 + 64:ppar * 128 + 128],
                                 PT[:, gpt, 512:1024], start=first, stop=last,
                                 tile_position=(0, 64), skip_group_check=True)
                        e.matmul(ps_den[opar][0:1, :], onescol[:], PT[:, gpt, 0:512],
                                 start=first, stop=last)
                        return e.matmul(ps_den[opar][64:65, :], onescol[:],
                                        PT[:, gpt, 512:1024], start=first, stop=last,
                                        tile_position=(0, 64), skip_group_check=True)
                    pv_done[(t, pk)] = st('pe', pvmm, waits=pv_waits, inc=('s_pe_pv', 1))

                if j == 1:
                    emit_norm_tail()
                    emit_norm_tail()

            for (t, nkc) in streams:
                opar = t % 2
                st('act', lambda e, opar=opar: e.activation(
                    rs_sb[0:1, opar, :], ps_den[opar][0:1, :], AF.Ln),
                    waits=[('s_pe_pv', pv_done[(t, nkc - 1)]),
                           ('s_pe_user', last_bcast_user[opar])],
                    inc=('s_chna', 1))
                ln1 = st('act', lambda e, opar=opar: e.activation(
                    rs_sb[64:65, opar, :], ps_den[opar][64:65, :], AF.Ln),
                    inc=('s_chna', 1))
                st('act', lambda e, opar=opar: e.activation(
                    rs_sb[0:1, opar, :], rs_sb[0:1, opar, :], AF.Exp, scale=-1.0),
                    waits=[('s_chna', ln1)], inc=('s_dve_rs', 1))
                rs1 = st('act', lambda e, opar=opar: e.activation(
                    rs_sb[64:65, opar, :], rs_sb[64:65, opar, :], AF.Exp, scale=-1.0),
                    inc=('s_dve_rs', 1))
                last_rs[opar] = rs1
                pending_norm.append((opar, p, t, rs1))

        kq_users_end[p] = mk.c['s_pe_user']
        pv_end_pair[p] = mk.c['s_pe_pv']

    emit_norm_tail()
    emit_norm_tail()
    on_final = mk.c['s_dve_on']
    plc_final = [('s_plc0', mk.c.get('s_plc0', 0)), ('s_plc1', mk.c.get('s_plc1', 0))]
    chna_p1 = mk.c.get('s_chna', 0)
    p1_dead = [('s_pe_pv', mk.c['s_pe_pv']), ('s_pe_user', mk.c['s_pe_user']),
               ('s_dve_on', on_final), ('s_mskd', mk.c.get('s_mskd', 0))] + plc_final
    _w1 = [('s_dve_on', on_final)]
    mk.tap('K2T', K2T[:], (128, SEQ), BF16, waits=_w1)
    mk.tap('Q2T', Q2T[:], (128, NQ), BF16, waits=_w1)
    mk.tap('V2', V2[:], (128, NTC, 256), BF16, waits=_w1)
    mk.tap('o_allT', o_allT[:], (128, NPAIR, NQ), BF16, waits=_w1)
    s_p1.close()
    s_xt.close()
    s_mk.close()

    mrow = SB('mrow', (1, 2, 512), F32, 'right')
    vrow = SB('vrow', (1, 2, 512), F32, 'right')
    srow = SB('srow', (1, 2, 512), F32, 'right')
    rrow = SB('rrow', (1, 2, 512), F32, 'right')
    wd_sb = SB('wd_sb', (128, NHC, NDC, 128), BF16, 'right')
    wd_snap = 0
    s_p2 = ExitStack()
    y1 = s_p2.enter_context(nc.sbuf_tensor('y1', [128, 2, NEC, 512], F32))
    wo_sb = s_p2.enter_context(nc.sbuf_tensor('wo_sb', [128, NEC, NPAIR, 128], BF16))
    yb_sb = s_p2.enter_context(nc.sbuf_tensor('yb_sb', [128, 2, 512], BF16))
    sq_sb = s_p2.enter_context(nc.sbuf_tensor('sq_sb', [128, 2, 512], BF16))
    tmp_sb = s_p2.enter_context(nc.sbuf_tensor('tmp_sb', [128, 512], F32))
    po = [s_p2.enter_context(nc.psum_tensor(f'po{i}', [128, 512], F32)) for i in range(4)]
    pbc = [s_p2.enter_context(nc.psum_tensor(f'pbc{i}', [128, 512], F32)) for i in range(2)]
    pstat_s = s_p2.enter_context(nc.psum_tensor('pstat_s', [1, 512], F32))
    pstat_q = s_p2.enter_context(nc.psum_tensor('pstat_q', [1, 512], F32))

    wo_user = [0, 0, 0, 0]
    po_free = [('s_dve2', 0)] * 4
    yb_free = [0, 0]
    y1_free = 0
    rows_chn_p = 0
    bcast_read = 0
    apply_chn = 0

    for ec in range(NEC):
        st('pool', lambda e, ec=ec: e.dma_start(wo_sb[:, ec], d['wo'][ec]),
           waits=([('s_pe_user', xT_last_user)] if ec == 0 else []),
           inc=('s_ldw2', 16))
    wo_total = mk.c['s_ldw2']
    st('pool', lambda e: e.nop(), waits=[('s_ldw2', wo_total)])

    evacs = {}
    p2snap = {}
    y1_free = [0, 0]
    for qt in range(5):
      qp = 1 - qt % 2
      if qt < 4:
        for ec in range(NEC):
            par = ec % 4

            def omm(e, ec=ec, par=par, qt=qt):
                inst = None
                for hdc in range(NPAIR):
                    inst = e.matmul(po[par][:], wo_sb[:, ec, hdc, :],
                                    o_allT[:, hdc, qt * 512:(qt + 1) * 512],
                                    start=(hdc == 0), stop=(hdc == NPAIR - 1))
                return inst
            ow = [('s_ldw2', wo_total), po_free[par]]
            if qt == 0 and ec == 0:
                ow += [('s_dve_on', on_final)] + plc_final
            wo_user[par] = st('pe', omm, waits=ow, inc=('s_pe2', 1))
            evacs[(qt, ec)] = st('dve', lambda e, ec=ec, par=par, qt=qt, qp=qp:
                e.scalar_tensor_tensor(
                    y1[:, qp, ec, :], po[par][:], bo_sb[:, ec:ec + 1],
                    xqT[:, ec, qt * 512:(qt + 1) * 512], ALU.add, ALU.add),
                waits=[('s_pe2', wo_user[par]), ('s_act2', y1_free[qp])],
                inc=('s_dve2', 1))
            po_free[par] = ('s_dve2', evacs[(qt, ec)])

        wd_snap = st('pool', lambda e, ci=qt: e.dma_start(
            wd_sb[:, ci * 6:(ci + 1) * 6], d['wd'][:, ci * 6:(ci + 1) * 6]),
            waits=(p1_dead if qt == 0 else []),
            inc=('s_ldw3', 16))
        st('pool', lambda e: e.nop(), waits=[('s_ldw3', wd_snap)])

      if qt >= 1:
        lq = qt - 1
        lp = 1 - lq % 2
        for ec in range(NEC):
            par = ec % 2
            st('act', lambda e, ec=ec, par=par, lp=lp: e.activation(
                yb_sb[:, par, :], y1[:, lp, ec, :], AF.Copy),
                waits=[('s_dve2', evacs[(lq, ec)]), ('s_pe2st', yb_free[par])],
                inc=('s_act2', 1))
            asnap = st('act', lambda e, ec=ec, par=par, lp=lp: e.activation(
                sq_sb[:, par, :], y1[:, lp, ec, :], AF.Square),
                inc=('s_act2', 1))

            def stmm(e, ec=ec, par=par):
                e.matmul(pstat_s[:], onescol[:], yb_sb[:, par, :],
                         start=(ec == 0), stop=(ec == NEC - 1))
                return e.matmul(pstat_q[:], onescol[:], sq_sb[:, par, :],
                                start=(ec == 0), stop=(ec == NEC - 1))
            w = [('s_act2', asnap)]
            if ec == 0:
                w.append(('s_chnd', rows_chn_p))
                if lq == 0:
                    w.append(('s_chna', chna_p1))
            yb_free[par] = st('pe', stmm, waits=w, inc=('s_pe2st', 1))
        y1_free[lp] = mk.c['s_act2']

        c1 = st('dve', lambda e: e.tensor_scalar(mrow[:, 0, :], pstat_s[:], 1.0 / EMB,
                                                 None, ALU.mult),
                waits=[('s_pe2st', yb_free[(NEC - 1) % 2]), ('s_pe2bc', bcast_read),
                       ('s_chnd', apply_chn)],
                inc=('s_chnd', 1))
        st('dve', lambda e: e.tensor_scalar(vrow[:, 0, :], pstat_q[:], 1.0 / EMB,
                                            None, ALU.mult), inc=('s_chnd', 1))
        c3 = st('dve', lambda e: e.tensor_mul(srow[:, 0, :], mrow[:, 0, :], mrow[:, 0, :]),
                waits=[('s_chnd', c1)], inc=('s_chnd', 1))
        c4 = st('dve', lambda e: e.tensor_sub(vrow[:, 0, :], vrow[:, 0, :], srow[:, 0, :]),
                waits=[('s_chnd', c3)], inc=('s_chnd', 1))
        rows_chn_p = st('dve', lambda e: e.tensor_scalar(vrow[:, 0, :], vrow[:, 0, :], 1e-5,
                                                         None, ALU.add),
                        waits=[('s_chnd', c4)], inc=('s_chnd', 1))
        sq_snap = st('act', lambda e: e.activation(srow[:, 0, :], vrow[:, 0, :], AF.Ln),
                     waits=[('s_chnd', rows_chn_p)], inc=('s_act2', 1))
        rr_chn = st('act', lambda e: e.activation(rrow[:, 0, :], srow[:, 0, :], AF.Exp,
                                                  scale=-0.5),
                    waits=[('s_act2', sq_snap)], inc=('s_chna', 1))

        def bc2(e):
            e.matmul(pbc[0][:], onesrf[0:1, :], mrow[:, 0, :], start=True, stop=True)
            return e.matmul(pbc[1][:], onesrf[0:1, :], rrow[:, 0, :], start=True, stop=True)
        bcast_read = st('pe', bc2,
                        waits=[('s_chna', rr_chn), ('s_chnd', apply_chn)],
                        inc=('s_pe2bc', 1))

        prev = 0
        for ec in range(NEC):
            sb = st('dve', lambda e, ec=ec, lp=lp: e.tensor_sub(
                tmp_sb[:], y1[:, lp, ec, :], pbc[0][:]),
                    waits=[('s_pe2bc', bcast_read), ('s_chnd', prev)],
                    inc=('s_chnd', 1))
            prev = st('dve', lambda e, ec=ec, lq=lq: e.tensor_mul(
                x1T[:, ec, lq * 512:(lq + 1) * 512], tmp_sb[:], pbc[1][:]),
                waits=[('s_chnd', sb)], inc=('s_chnd', 1))
        apply_chn = prev
        p2snap[lq] = (apply_chn, y1_free[0], y1_free[1])

    omm_end = mk.c['s_pe2']
    dve2_final = mk.c['s_dve2']
    pe2st_final = mk.c['s_pe2st']
    act2_final = mk.c['s_act2']
    chna_final = mk.c['s_chna']
    bcast_final = bcast_read
    apply_final = apply_chn
    mk.tap('x1T', x1T[:], (128, NPAIR, NQ), BF16,
           waits=[('s_chnd', mk.c.get('s_chnd', 0)), ('s_chna', mk.c.get('s_chna', 0)), ('s_dve2', mk.c['s_dve2'])])
    s_p2.close()
    s_xq.close()

    s_p3 = ExitStack()
    hT = s_p3.enter_context(nc.sbuf_tensor('hT', [128, NHC, 1024], BF16))
    wgb = s_p3.enter_context(nc.sbuf_tensor('wgb', [128, 4, NEC, 128], BF16))
    wub = s_p3.enter_context(nc.sbuf_tensor('wub', [128, 4, NEC, 128], BF16))
    gs_sb = s_p3.enter_context(nc.sbuf_tensor('gs_sb', [128, 2, 512], BF16))
    y2 = s_p3.enter_context(nc.sbuf_tensor('y2', [128, NEC, 1024], F32))
    yb3 = s_p3.enter_context(nc.sbuf_tensor('yb3', [128, 2, 512], BF16))
    sq3 = s_p3.enter_context(nc.sbuf_tensor('sq3', [128, 2, 512], BF16))
    tmp3 = s_p3.enter_context(nc.sbuf_tensor('tmp3', [128, 512], F32))
    pA = s_p3.enter_context(nc.psum_tensor('pA', [128, 512], F32))
    pB = s_p3.enter_context(nc.psum_tensor('pB', [128, 512], F32))
    pC = s_p3.enter_context(nc.psum_tensor('pC', [128, 512], F32))
    pD = s_p3.enter_context(nc.psum_tensor('pD', [128, 512], F32))
    pst = [[s_p3.enter_context(nc.psum_tensor(f'pst{h}{i}', [128, 512], F32))
            for i in range(2)] for h in range(2)]
    pg = [pA, pB]
    pu = [pC, pD]
    pd = [pA, pB]


    wgu_snap = {}
    wgu_free = [0, 0, 0, 0]
    silu_free = [0, 0]
    hm_free = [0, 0]
    down_end = 0
    y2ev_free = [0, 0]
    stat_free = [0, 0]
    yb3_free = [0, 0]
    y2_free = 0
    apply3 = 0
    bc3_read = 0
    st_dma = 0

    def load_wgu(qt2, s):
        h0, h1 = 2 * s, 2 * s + 1
        gate = max(wgu_free[h0 % 4], wgu_free[h1 % 4])
        w0 = [('s_pe3g', gate)]
        if qt2 == 0 and s == 0:
            w0 += [('s_act2', p2snap[2][2]), ('s_chnd', p2snap[2][0])]
        st('pool', lambda e, hc=h0: e.dma_start(wgb[:, hc % 4], d['wg'][hc]),
           waits=w0, inc=('s_ldw3', 16))
        st('pool', lambda e, hc=h0: e.dma_start(wub[:, hc % 4], d['wu'][hc]),
           inc=('s_ldw3', 16))
        st('pool', lambda e, hc=h1: e.dma_start(wgb[:, hc % 4], d['wg'][hc]),
           inc=('s_ldw3', 16))
        snap = st('pool', lambda e, hc=h1: e.dma_start(wub[:, hc % 4], d['wu'][hc]),
                  inc=('s_ldw3', 16))
        wgu_snap[(qt2, h0)] = wgu_snap[(qt2, h1)] = snap
        st('pool', lambda e: e.nop(), waits=[('s_ldw3', snap)])

    for qt2 in range(2):
        q0 = qt2 * 1024
        load_wgu(qt2, 0)
        load_wgu(qt2, 1)
        hmults = []
        for hc in range(NHC):
            par = hc % 2
            for half in range(2):
                w = 2 * hc + half
                wp = w % 2
                q1 = q0 + half * 512

                def gmm(e, par=hc % 4, wp=wp, q1=q1):
                    inst = None
                    for ec in range(NEC):
                        inst = e.matmul(pg[wp][:], wgb[:, par, ec, :],
                                        x1T[:, ec, q1:q1 + 512],
                                        start=(ec == 0), stop=(ec == NEC - 1))
                    return inst
                gw = [('s_ldw3', wgu_snap[(qt2, hc)]),
                      ('s_act3', silu_free[wp]), ('s_pe3d', down_end),
                      ('s_dve3', y2ev_free[wp])]
                if hc == 0 and half == 0:
                    gw += [('s_chnd', p2snap[2 * qt2 + 1][0]),
                           ('s_dve2', dve2_final)]
                gsnap = st('pe', gmm, waits=gw, inc=('s_pe3g', 1))

                def umm(e, par=hc % 4, wp=wp, q1=q1):
                    inst = None
                    for ec in range(NEC):
                        inst = e.matmul(pu[wp][:], wub[:, par, ec, :],
                                        x1T[:, ec, q1:q1 + 512],
                                        start=(ec == 0), stop=(ec == NEC - 1))
                    return inst
                uw = [('s_dve3', hm_free[wp]), ('s_chnd', apply3)]
                if qt2 == 0 and hc == 0 and half == 0:
                    uw += [('s_dve2', dve2_final)]
                usnap = st('pe', umm, waits=uw, inc=('s_pe3g', 1))
                slw = [('s_pe3g', gsnap), ('s_dve3', hm_free[wp])]
                if qt2 == 0 and hc == 0 and half == 0:
                    slw += [('s_chnd', p2snap[2][0])]
                silu_free[wp] = st('act', lambda e, wp=wp: e.activation(
                    gs_sb[:, wp, :], pg[wp][:], AF.Silu, scale=1.0 / 127.0),
                    waits=slw, inc=('s_act3', 1))
                hw = [('s_pe3g', usnap), ('s_act3', silu_free[wp])]
                if qt2 == 0 and hc == 16 and half == 0:
                    hw += [('s_act2', p2snap[3][1])]
                hm_free[wp] = st('dve', lambda e, wp=wp, hc=hc, half=half: e.tensor_mul(
                    hT[:, hc, half * 512:(half + 1) * 512], pu[wp][:], gs_sb[:, wp, :]),
                    waits=hw, inc=('s_dve3', 1))
                hmults.append(hm_free[wp])
            wgu_free[hc % 4] = mk.c['s_pe3g']
            if hc % 2 == 1 and hc + 3 < NHC:
                load_wgu(qt2, (hc + 3) // 2)
        hT_ready = hmults[-1]

        ev3 = {}
        ybsq = {}
        for half in range(2):
            for dc in range(NDC):
                dw = half * NDC + dc
                dp = dw % 2
                q1 = q0 + half * 512

                def dmm(e, dc=dc, half=half, dp=dp):
                    inst = None
                    for hc in range(NHC):
                        inst = e.matmul(pd[dp][:], wd_sb[:, hc, dc, :],
                                        hT[:, hc, half * 512:(half + 1) * 512],
                                        start=(hc == 0), stop=(hc == NHC - 1))
                    return inst
                w = [('s_dve3', hT_ready), ('s_ldw3', wd_snap),
                     ('s_dve3', y2ev_free[dp]), ('s_act3', silu_free[dp])]
                dsnap = st('pe', dmm, waits=w, inc=('s_pe3d', 1))
                ew = [('s_pe3d', dsnap), ('s_out', st_dma), ('s_act3', y2_free)]
                if qt2 == 0 and half == 0 and dc == 0:
                    ew += [('s_act2', p2snap[2][2]), ('s_pe2st', pe2st_final)]
                ev3[(dc, half)] = st('dve', lambda e, dc=dc, half=half, dp=dp, q1=q1:
                    e.scalar_tensor_tensor(
                        y2[:, dc, half * 512:(half + 1) * 512], pd[dp][:], FS2,
                        x1T[:, dc, q1:q1 + 512], ALU.mult, ALU.add),
                    waits=ew, inc=('s_dve3', 1))
                y2ev_free[dp] = ev3[(dc, half)]
        down_end = mk.c['s_pe3d']

        for half in range(2):
            for dc in range(NDC):
                par = dc % 2
                st('act', lambda e, dc=dc, par=par, half=half: e.activation(
                    yb3[:, par, :], y2[:, dc, half * 512:(half + 1) * 512], AF.Copy),
                    waits=[('s_dve3', ev3[(dc, half)]), ('s_pe3st', yb3_free[par])],
                    inc=('s_act3', 1))
                asnap = st('act', lambda e, dc=dc, par=par, half=half: e.activation(
                    sq3[:, par, :], y2[:, dc, half * 512:(half + 1) * 512], AF.Square),
                    inc=('s_act3', 1))

                def stmm3(e, dc=dc, par=par, half=half):
                    e.matmul(pst[half][0][0:1, :], onescol[:], yb3[:, par, :],
                             start=(dc == 0), stop=(dc == NDC - 1))
                    return e.matmul(pst[half][1][0:1, :], onescol[:], sq3[:, par, :],
                                    start=(dc == 0), stop=(dc == NDC - 1))
                w = [('s_act3', asnap)]
                if dc == 0:
                    w += [('s_chnd', stat_free[half]), ('s_dve3', hm_free[0]),
                          ('s_dve3', hm_free[1]), ('s_chnd', apply3)]
                    if qt2 == 0 and half == 0:
                        w += [('s_chnd', apply_final)]
                yb3_free[par] = st('pe', stmm3, waits=w, inc=('s_pe3st', 1))
            ybsq[half] = mk.c['s_pe3st']
        y2_free = mk.c['s_act3']

        rr3h = [0, 0]
        for half in range(2):
            cw = [('s_pe3st', ybsq[half]), ('s_pe3bc', bc3_read)]
            if qt2 == 0 and half == 0:
                cw += [('s_pe2bc', bcast_final), ('s_chna', chna_final),
                       ('s_act2', act2_final)]
            c1 = st('dve', lambda e, half=half: e.tensor_scalar(
                mrow[:, half, :], pst[half][0][0:1, :], 1.0 / EMB, None, ALU.mult),
                    waits=cw, inc=('s_chnd', 1))
            st('dve', lambda e, half=half: e.tensor_scalar(
                vrow[:, half, :], pst[half][1][0:1, :], 1.0 / EMB, None, ALU.mult),
               inc=('s_chnd', 1))
            c3 = st('dve', lambda e, half=half: e.tensor_mul(
                srow[:, half, :], mrow[:, half, :], mrow[:, half, :]),
                    waits=[('s_chnd', c1)], inc=('s_chnd', 1))
            c4 = st('dve', lambda e, half=half: e.tensor_sub(
                vrow[:, half, :], vrow[:, half, :], srow[:, half, :]),
                    waits=[('s_chnd', c3)], inc=('s_chnd', 1))
            stat_free[half] = st('dve', lambda e, half=half: e.tensor_scalar(
                vrow[:, half, :], vrow[:, half, :], 1e-5, None, ALU.add),
                           waits=[('s_chnd', c4)], inc=('s_chnd', 1))
            sq_snap3 = st('act', lambda e, half=half: e.activation(
                srow[:, half, :], vrow[:, half, :], AF.Ln),
                          waits=[('s_chnd', stat_free[half])], inc=('s_act3', 1))
            rr3h[half] = st('act', lambda e, half=half: e.activation(
                rrow[:, half, :], srow[:, half, :], AF.Exp, scale=-0.5),
                     waits=[('s_act3', sq_snap3)], inc=('s_chna', 1))

        for half in range(2):
            q1 = q0 + half * 512
            bcb = [pC, pD] if half == 0 else [pst[0][0], pst[0][1]]

            def bc3f(e, half=half, bcb=bcb):
                e.matmul(bcb[0][:], onesrf[0:1, :], mrow[:, half, :],
                         start=True, stop=True, skip_group_check=True)
                return e.matmul(bcb[1][:], onesrf[0:1, :], rrow[:, half, :],
                                start=True, stop=True, skip_group_check=True)
            bw = [('s_chna', rr3h[half])]
            if half == 0:
                bw += [('s_dve3', hm_free[0]), ('s_dve3', hm_free[1]),
                       ('s_chnd', apply3)]
            else:
                bw += [('s_chnd', stat_free[0])]
            bc3_read = st('pe', bc3f, waits=bw, inc=('s_pe3bc', 1))

            prev3 = apply3
            for dc in range(NDC):
                sb3 = st('dve', lambda e, dc=dc, half=half, bcb=bcb: e.tensor_sub(
                    tmp3[:], y2[:, dc, half * 512:(half + 1) * 512], bcb[0][:]),
                   waits=[('s_pe3bc', bc3_read), ('s_chnd', prev3)],
                   inc=('s_chnd', 1))
                prev3 = st('dve', lambda e, dc=dc, half=half, bcb=bcb: e.tensor_mul(
                    y2[:, dc, half * 512:(half + 1) * 512], tmp3[:], bcb[1][:]),
                    waits=[('s_chnd', sb3)], inc=('s_chnd', 1))
            apply3 = prev3
            ydst = y.rearrange("(ec p) q -> p ec q", p=128)[:, :, q1:q1 + 512]
            st_dma = st('sp', lambda e, ydst=ydst, half=half: e.dma_start(
                ydst, y2[:, :, half * 512:(half + 1) * 512]),
                waits=[('s_chnd', apply3)], inc=('s_out', 16))
            st('sp', lambda e: e.nop(), waits=[('s_out', st_dma)])

    st('sp', lambda e: e.nop(), waits=[('s_out', mk.c['s_out'])])
    _w3 = [('s_chnd', mk.c.get('s_chnd', 0)), ('s_chna', mk.c.get('s_chna', 0)), ('s_dve3', mk.c['s_dve3'])]
    mk.tap('hT', hT[:], (128, NHC, 1024), BF16, waits=_w3)
    mk.tap('y2', y2[:], (128, NEC, 1024), F32, waits=_w3)
    mk.flush()
    s_p3.close()
    mk.es.close()
    return mk



def _prep_shared(inp):
    bf = ml_dtypes.bfloat16
    Wq = np.asarray(inp['L0_attn_Wq'], np.float32)
    Wk = np.asarray(inp['L0_attn_Wk'], np.float32)
    Wv = np.asarray(inp['L0_attn_Wv'], np.float32)
    Wo = np.asarray(inp['L0_attn_Wo'], np.float32)
    bq = np.asarray(inp['L0_attn_bq'], np.float32)
    bk = np.asarray(inp['L0_attn_bk'], np.float32)
    bv = np.asarray(inp['L0_attn_bv'], np.float32)
    bo = np.asarray(inp['L0_attn_bo'], np.float32)
    wg = np.asarray(inp['L0_ffn_w_gate'], np.int32)
    wu = np.asarray(inp['L0_ffn_w_up'], np.int32)
    wd = np.asarray(inp['L0_ffn_w_down'], np.int32)

    def qk_tiles(W):
        fm = np.ascontiguousarray(np.transpose(W, (1, 0, 2)).reshape(EMB, EMB))
        out = np.empty((NPAIR, 128, NEC, 128), np.float32)
        for p in range(NPAIR):
            blk = fm[:, 128 * p:128 * (p + 1)]
            out[p] = blk.reshape(NEC, 128, 128).transpose(1, 0, 2)
        return np.ascontiguousarray(out)

    sh = {}
    sh['wq'] = qk_tiles(Wq)
    sh['wk'] = qk_tiles(Wk)
    vfm = np.transpose(Wv, (1, 0, 2)).reshape(EMB, EMB)
    v1 = vfm.reshape(NEC, 128, EMB).transpose(1, 0, 2)
    sh['wv'] = np.ascontiguousarray(
        v1.reshape(128, NEC, 4, 256).transpose(2, 0, 1, 3))
    sh['wo'] = np.ascontiguousarray(
        Wo.reshape(NPAIR, 128, NEC, 128).transpose(2, 1, 0, 3))
    sh['bq'] = np.ascontiguousarray(bq.reshape(NPAIR, 128).T)
    sh['bk'] = np.ascontiguousarray(bk.reshape(NPAIR, 128).T)
    sh['bo'] = np.ascontiguousarray(bo.reshape(NEC, 128).T)
    sh['bv'] = np.ascontiguousarray(bv.reshape(1, EMB))
    sh['wg'] = np.ascontiguousarray(
        wg.reshape(NHC, 128, NEC, 128).transpose(0, 3, 2, 1))
    sh['wu'] = np.ascontiguousarray(
        wu.reshape(NHC, 128, NEC, 128).transpose(0, 3, 2, 1))
    sh['wd'] = np.ascontiguousarray(
        wd.reshape(NDC, 128, NHC, 128).transpose(3, 2, 0, 1))
    sh['ident'] = np.eye(128, dtype=bf)
    sh['ones_col'] = np.ones((128, 1), bf)
    sh['ones_row'] = np.ones((128, 128), bf)
    sh['ones_rf'] = np.ones((128, 128), np.float32)
    return sh


def _masks_for_role(role):
    out = np.zeros((4, 8, 128, 512), np.uint8)
    ii = np.arange(128)[:, None]
    jj = np.arange(512)[None, :]
    for t in range(4):
        q0 = 512 * TILES[role][t]
        for rel in range(8):
            kc = CT[t] - 8 + rel
            out[t, rel] = ((128 * kc + ii) <= (q0 + jj)).astype(np.uint8)
    return np.ascontiguousarray(out.transpose(2, 0, 1, 3))


def kernel(**inputs):
    inp = {k: np.asarray(v) for k, v in inputs.items()}
    x = np.asarray(inp['L0_attn_x'], np.float32)
    sh = _prep_shared(inp)
    mk_role = [_masks_for_role(0), _masks_for_role(1)]

    in_maps = []
    for c in range(8):
        s, r = c // 2, c % 2
        xs = np.ascontiguousarray(x[s])
        xq = np.ascontiguousarray(
            np.concatenate([xs[512 * j:512 * (j + 1)] for j in TILES[r]], axis=0))
        m = dict(sh)
        m['x'] = xs
        m['xq'] = xq
        m['mk'] = mk_role[r]
        in_maps.append(m)

    mk = build_full()
    res = run_bass_kernel_spmd(mk.nc, in_maps, core_ids=list(range(8)))

    out = np.empty((4, SEQ, EMB), np.float32)
    for c in range(8):
        s, r = c // 2, c % 2
        yT = np.asarray(res.results[c]['y']).T
        for t, j in enumerate(TILES[r]):
            out[s, 512 * j:512 * (j + 1)] = yT[512 * t:512 * (t + 1)]
    return out
